# version 48
# baseline (speedup 1.0000x reference)
import numpy as np
import contextlib
import concourse.bass as bass
import concourse.mybir as mybir
from concourse.bass_utils import run_bass_kernel_spmd

F32 = mybir.dt.float32
BF16 = mybir.dt.bfloat16
AF = mybir.ActivationFunctionType
ALU = mybir.AluOpType

SHIFT = 12.0
SCALE = 1.0 / 32.0
R = 13
RING = 8
NG = 16
NCHUNK = 16
EPS = 1e-5

STREAM = [(g, kb) for g in range(NG) for kb in range(R, 4 * g + 4)]
SI = {gk: i for i, gk in enumerate(STREAM)}
NSTREAM_BLOCKS = 64 - R


def n_av_before(g):
    return 2 * g * g + 2 * g


def av_idx(g, kb):
    return n_av_before(g) + kb + 1


def build_program():
    nc = bass.Bass("TRN2", target_bir_lowering=False)

    x_seq = nc.dram_tensor("x_seq", [8192, 1024], F32, kind="ExternalInput")
    x_own = nc.dram_tensor("x_own", [4096, 1024], F32, kind="ExternalInput")
    wq_d = nc.dram_tensor("wq", [1024, 1024], F32, kind="ExternalInput")
    wk_d = nc.dram_tensor("wk", [1024, 1024], F32, kind="ExternalInput")
    wv_d = nc.dram_tensor("wv", [1024, 1024], F32, kind="ExternalInput")
    wo_d = nc.dram_tensor("wo", [1024, 1024], F32, kind="ExternalInput")
    u1_d = nc.dram_tensor("u1", [1024, 256], F32, kind="ExternalInput")
    v1_d = nc.dram_tensor("v1", [256, 4096], F32, kind="ExternalInput")
    b1_d = nc.dram_tensor("b1t", [128, 32], F32, kind="ExternalInput")
    u2_d = nc.dram_tensor("u2", [4096, 256], F32, kind="ExternalInput")
    v2_d = nc.dram_tensor("v2", [256, 1024], F32, kind="ExternalInput")
    b2_d = nc.dram_tensor("b2b", [128, 1024], F32, kind="ExternalInput")
    msk_d = nc.dram_tensor("masks", [128, 1024], F32, kind="ExternalInput")
    id_d = nc.dram_tensor("ident", [128, 128], F32, kind="ExternalInput")
    out_d = nc.dram_tensor("out", [4096, 1024], F32, kind="ExternalOutput")

    xbf = nc.dram_tensor("xbf", [8192, 1024], BF16, kind="Internal")
    xobf = nc.dram_tensor("xobf", [4096, 1024], BF16, kind="Internal")
    qscr = nc.dram_tensor("qscr", [NG, 128, 2048], BF16, kind="Internal")
    kscr = nc.dram_tensor("kscr", [NSTREAM_BLOCKS, 128, 1024], BF16, kind="Internal")
    vscr = nc.dram_tensor("vscr", [NSTREAM_BLOCKS, 128, 1024], BF16, kind="Internal")

    x_seq_a, x_own_a = x_seq.ap(), x_own.ap()
    out_a = out_d.ap()
    xbf_a, xobf_a = xbf.ap(), xobf.ap()
    qscr_a, kscr_a, vscr_a = qscr.ap(), kscr.ap(), vscr.ap()

    ctx = contextlib.ExitStack()
    with ctx:
        E = ctx.enter_context
        idf = E(nc.sbuf_tensor("idf", [128, 128], F32))
        ones_b = E(nc.sbuf_tensor("ones_b", [128, 1], BF16))
        ones_r = E(nc.sbuf_tensor("ones_r", [128, 128], F32))
        shift_s = E(nc.sbuf_tensor("shift_s", [128, 1], F32))
        eps_s = E(nc.sbuf_tensor("eps_s", [128, 1], F32))
        msk_sb = E(nc.sbuf_tensor("msk_sb", [128, 1024], BF16))
        b1_sb = E(nc.sbuf_tensor("b1_sb", [128, 32], F32))
        b2_sb = E(nc.sbuf_tensor("b2_sb", [128, 1024], F32))
        u1_sb = E(nc.sbuf_tensor("u1_sb", [128, 2048], BF16))
        v2_sb = E(nc.sbuf_tensor("v2_sb", [128, 2048], BF16))
        kres = E(nc.sbuf_tensor("kres", [128, 8 * R * 128], BF16))
        vres = E(nc.sbuf_tensor("vres", [128, R * 1024], BF16))

        ps_s = [E(nc.psum_tensor(f"ps_s{i}", [128, 512], F32)) for i in range(2)]
        ps_a = [E(nc.psum_tensor(f"ps_a{i}", [128, 1024], F32)) for i in range(2)]
        ps_x = [E(nc.psum_tensor(f"ps_x{i}", [128, 512], F32)) for i in range(2)]

        S = lambda name: E(nc.semaphore(name))
        sem_w = S("sem_w")
        sem_wk1 = S("sem_wk1")
        sem_wv1 = S("sem_wv1")
        sem_wq1 = S("sem_wq1")
        sem_ms = S("sem_ms")
        gc = [S(f"gc{i}") for i in range(4)]
        str2 = [S(f"str{i}") for i in range(2)]
        str_e = [[S(f"se{p}_{e}") for e in range(8)] for p in range(2)]
        t_k, t_v, t_q = S("t_k"), S("t_v"), S("t_q")
        v_k, v_v, v_q = S("v_k"), S("v_v"), S("v_q")
        dk2 = [S(f"dk{i}") for i in range(2)]
        dv2 = [S(f"dv{i}") for i in range(2)]
        dq2 = [S(f"dq{i}") for i in range(2)]

        sem_w2 = S("sem_w2")
        sem_w3 = S("sem_w3")
        qsem = [S(f"qsem{i}") for i in range(2)]
        sem_xo = S("sem_xo")
        kstS = [S(f"kstS{i}") for i in range(RING)]
        vstS = [S(f"vstS{i}") for i in range(RING)]
        sem_score = S("sem_score")
        sem_exp = S("sem_exp")
        sem_msk = S("sem_msk")
        sem_av = S("sem_av")
        sem_ge = S("sem_ge")
        sem_gae = S("sem_gae")
        sem_q2p = S("sem_q2p")
        sem_trp = S("sem_trp")
        sem_trpe = S("sem_trpe")
        sem_trpp = S("sem_trpp")
        sem_wo = S("sem_wo")
        sem_r1h = S("sem_r1h")
        sem_sq1 = S("sem_sq1")
        sem_va = S("sem_va")
        sem_sd = S("sem_sd")
        sem_st1 = S("sem_st1")
        sem_stt = S("sem_stt")
        sem_ste = S("sem_ste")
        sem_rep = S("sem_rep")
        sem_u1c = S("sem_u1c")
        sem_u1ce = S("sem_u1ce")
        sem_xt = S("sem_xt")
        sem_xte = S("sem_xte")
        sem_p1 = S("sem_p1")
        sem_p1e = S("sem_p1e")
        sem_h = S("sem_h")
        sem_gelu = S("sem_gelu")
        sem_q2 = S("sem_q2")
        sem_q2e = S("sem_q2e")
        sem_f = S("sem_f")
        sem_r2h = S("sem_r2h")
        sem_sq2 = S("sem_sq2")
        sem_vb = S("sem_vb")
        sem_sd2 = S("sem_sd2")
        sem_st2 = S("sem_st2")
        sem_ln2 = S("sem_ln2")
        outd = [S(f"outd{i}") for i in range(2)]

        p1ctx = contextlib.ExitStack()
        with p1ctx:
            E1 = p1ctx.enter_context
            wq_sb = E1(nc.sbuf_tensor("wq_sb", [128, 8192], BF16))
            wk_sb = E1(nc.sbuf_tensor("wk_sb", [128, 8192], BF16))
            wv_sb = E1(nc.sbuf_tensor("wv_sb", [128, 8192], BF16))
            xt = E1(nc.sbuf_tensor("xt", [128, 2 * 4096], BF16))
            xot = E1(nc.sbuf_tensor("xot", [128, 2 * 2048], BF16))
            kstage = E1(nc.sbuf_tensor("kstage", [128, 2 * 4096], BF16))
            vstage = E1(nc.sbuf_tensor("vstage", [128, 2 * 1024], BF16))
            qstage = E1(nc.sbuf_tensor("qstage", [128, 2 * 2048], BF16))

            blk1 = E1(nc.Block("p1"))

            @blk1.gpsimd
            def _(g):
                g.memset(ones_b[:, :], 1.0)
                g.memset(ones_r[0:1, :], 1.0)
                g.memset(shift_s[:, :], -SHIFT)
                g.memset(eps_s[:, :], EPS)
                g.drain().then_inc(sem_ms, 1)

                def cast_early(tc):
                    g.dma_start(out=xbf_a[tc * 512:(tc + 1) * 512, :],
                                in_=x_seq_a[tc * 512:(tc + 1) * 512, :]).then_inc(gc[tc % 4], 16)
                    g.dma_start(out=xobf_a[tc * 256:(tc + 1) * 256, :],
                                in_=x_own_a[tc * 256:(tc + 1) * 256, :]).then_inc(gc[tc % 4], 16)
                cast_early(0)
                cast_early(1)
                for ec in range(8):
                    g.dma_start(out=wk_sb[:, ec * 1024:(ec + 1) * 1024],
                                in_=wk_d.ap()[ec * 128:(ec + 1) * 128, :]).then_inc(sem_wk1, 16)
                for ec in range(8):
                    g.dma_start(out=wv_sb[:, ec * 1024:(ec + 1) * 1024],
                                in_=wv_d.ap()[ec * 128:(ec + 1) * 128, :]).then_inc(sem_wv1, 16)
                for ec in range(8):
                    g.dma_start(out=wq_sb[:, ec * 1024:(ec + 1) * 1024],
                                in_=wq_d.ap()[ec * 128:(ec + 1) * 128, :]).then_inc(sem_wq1, 16)
                for dc in range(8):
                    g.dma_start(out=u1_sb[:, dc * 256:(dc + 1) * 256],
                                in_=u1_d.ap()[dc * 128:(dc + 1) * 128, :]).then_inc(sem_w, 16)
                for rc in range(2):
                    g.dma_start(out=v2_sb[:, rc * 1024:(rc + 1) * 1024],
                                in_=v2_d.ap()[rc * 128:(rc + 1) * 128, :]).then_inc(sem_w, 16)
                g.dma_start(out=b1_sb[:, :], in_=b1_d.ap()[:, :]).then_inc(sem_w, 16)
                g.dma_start(out=b2_sb[:, :], in_=b2_d.ap()[:, :]).then_inc(sem_w, 16)
                g.dma_start(out=msk_sb[:, :], in_=msk_d.ap()[:, :]).then_inc(sem_w, 16)
                g.dma_start(out=idf[:, :], in_=id_d.ap()[:, :]).then_inc(sem_w, 16)
                cast = cast_early
                for tc in range(NCHUNK):
                    if tc + 2 < NCHUNK:
                        cast(tc + 2)
                    if tc >= 3:
                        g.wait_ge(v_k, 8 * (tc + 1))
                        for j in range(4):
                            kb = 4 * tc + j
                            if kb < R:
                                continue
                            off = (tc % 2) * 4096 + j * 1024
                            g.dma_start(out=kscr_a[kb - R],
                                        in_=kstage[:, off:off + 1024]).then_inc(dk2[tc % 2], 16)
                        for tt in range(4):
                            kb = 4 * tc + tt
                            if kb < R:
                                continue
                            g.wait_ge(v_v, 4 * tc + tt + 1)
                            g.dma_start(
                                out=vscr_a[kb - R],
                                in_=vstage[:, (kb % 2) * 1024:((kb % 2) + 1) * 1024],
                            ).then_inc(dv2[kb % 2], 16)
                    g.wait_ge(v_q, 8 * (tc + 1))
                    g.dma_start(out=qscr_a[tc],
                                in_=qstage[:, (tc % 2) * 2048:((tc % 2) + 1) * 2048],
                                ).then_inc(dq2[tc % 2], 16)

            N0 = 24 + 8 + 2 + 4

            @blk1.sync
            def _(s):
                for tc in range(NCHUNK):
                    s.wait_ge(gc[tc % 4], 32 * (tc // 4 + 1))
                    if tc >= 2:
                        s.wait_ge(t_v, 4 * (tc - 1))
                    for ec in range(8):
                        s.dma_start(
                            out=xt[:, (tc % 2) * 4096 + ec * 512:(tc % 2) * 4096 + (ec + 1) * 512],
                            in_=xbf_a[tc * 512:(tc + 1) * 512, ec * 128:(ec + 1) * 128],
                            transpose=True).then_inc(str_e[tc % 2][ec], 16)
                    if tc >= 2:
                        s.wait_ge(t_q, 8 * (tc - 1))
                    for ec in range(8):
                        s.dma_start(
                            out=xot[:, (tc % 2) * 2048 + ec * 256:(tc % 2) * 2048 + (ec + 1) * 256],
                            in_=xobf_a[tc * 256:(tc + 1) * 256, ec * 128:(ec + 1) * 128],
                            transpose=True).then_inc(str2[tc % 2], 16)

            @blk1.tensor
            def _(t):
                t.wait_ge(sem_ms, 1)
                t.wait_ge(sem_wk1, 16 * 8)
                for tc in range(NCHUNK):
                    xb = (tc % 2) * 4096
                    xob = (tc % 2) * 2048
                    for dco in range(8):
                        u = 8 * tc + dco
                        if u >= 2:
                            t.wait_ge(v_k, u - 1)
                        for ec in range(8):
                            if dco == 0:
                                t.wait_ge(str_e[tc % 2][ec], 16 * (tc // 2 + 1))
                            mm = t.matmul(ps_s[u % 2][:, 0:512],
                                          wk_sb[:, ec * 1024 + dco * 128:ec * 1024 + (dco + 1) * 128],
                                          xt[:, xb + ec * 512:xb + (ec + 1) * 512],
                                          start=(ec == 0), stop=(ec == 7))
                        mm.then_inc(t_k, 1)
                    if tc == 0:
                        t.wait_ge(sem_wv1, 16 * 8)
                    for tt in range(4):
                        u = 4 * tc + tt
                        if u >= 2:
                            t.wait_ge(v_v, u - 1)
                        for ec in range(8):
                            lhs = xt[:, xb + ec * 512 + tt * 128:xb + ec * 512 + (tt + 1) * 128]
                            t.matmul(ps_a[u % 2][:, 0:512], lhs, wv_sb[:, ec * 1024:ec * 1024 + 512],
                                     start=(ec == 0), stop=(ec == 7))
                            mm = t.matmul(ps_a[u % 2][:, 512:1024], lhs,
                                          wv_sb[:, ec * 1024 + 512:(ec + 1) * 1024],
                                          start=(ec == 0), stop=(ec == 7))
                        mm.then_inc(t_v, 1)
                    if tc == 0:
                        t.wait_ge(sem_wq1, 16 * 8)
                    t.wait_ge(str2[tc % 2], 128 * (tc // 2 + 1))
                    for dco in range(8):
                        u = 8 * tc + dco
                        if u >= 2:
                            t.wait_ge(v_q, u - 1)
                        for ec in range(8):
                            mm = t.matmul(ps_x[u % 2][:, 0:256],
                                          wq_sb[:, ec * 1024 + dco * 128:ec * 1024 + (dco + 1) * 128],
                                          xot[:, xob + ec * 256:xob + (ec + 1) * 256],
                                          start=(ec == 0), stop=(ec == 7))
                        mm.then_inc(t_q, 1)

            @blk1.vector
            def _(v):
                for tc in range(NCHUNK):
                    def k_dmas_of(c):
                        return len([j for j in range(4) if 4 * c + j >= R])

                    for dco in range(8):
                        v.wait_ge(t_k, 8 * tc + dco + 1)
                        if tc < 3:
                            v.tensor_copy(
                                kres[:, dco * (R * 128) + tc * 512: dco * (R * 128) + (tc + 1) * 512],
                                ps_s[(8 * tc + dco) % 2][:, 0:512]).then_inc(v_k, 1)
                        else:
                            if dco == 0 and tc >= 5:
                                nprior = sum(16 * k_dmas_of(c)
                                             for c in range(3, tc - 1) if c % 2 == tc % 2)
                                v.wait_ge(dk2[tc % 2], nprior)
                            base = (tc % 2) * 4096 + dco * 128
                            cp = None
                            for j in range(4):
                                kb = 4 * tc + j
                                if kb < R:
                                    dst = kres[:, dco * (R * 128) + kb * 128:
                                               dco * (R * 128) + (kb + 1) * 128]
                                else:
                                    dst = kstage[:, base + j * 1024:base + j * 1024 + 128]
                                cp = v.tensor_copy(
                                    dst, ps_s[(8 * tc + dco) % 2][:, j * 128:(j + 1) * 128])
                            cp.then_inc(v_k, 1)
                    for tt in range(4):
                        kb = 4 * tc + tt
                        v.wait_ge(t_v, kb + 1)
                        if kb < R:
                            dst = vres[:, kb * 1024:(kb + 1) * 1024]
                        else:
                            if kb - 2 >= R:
                                first_p = R if R % 2 == kb % 2 else R + 1
                                n_writes = (kb - 2 - first_p) // 2 + 1
                                v.wait_ge(dv2[kb % 2], 16 * n_writes)
                            dst = vstage[:, (kb % 2) * 1024:((kb % 2) + 1) * 1024]
                        v.tensor_copy(dst, ps_a[kb % 2][:, 0:1024]).then_inc(v_v, 1)
                    for dco in range(8):
                        v.wait_ge(t_q, 8 * tc + dco + 1)
                        if dco == 0 and tc >= 2:
                            v.wait_ge(dq2[tc % 2], 16 * (tc // 2))
                        v.tensor_copy(qstage[:, (tc % 2) * 2048 + dco * 256:(tc % 2) * 2048 + (dco + 1) * 256],
                                      ps_x[(8 * tc + dco) % 2][:, 0:256]).then_inc(v_q, 1)

        p2ctx = contextlib.ExitStack()
        with p2ctx:
            E2 = p2ctx.enter_context
            wo_sb = E2(nc.sbuf_tensor("wo_sb", [128, 8192], BF16))
            v1_sb = E2(nc.sbuf_tensor("v1_sb", [128, 8192], BF16))
            u2_sb = E2(nc.sbuf_tensor("u2_sb", [128, 8192], BF16))
            kst = E2(nc.sbuf_tensor("kst", [128, RING * 1024], BF16))
            vst = E2(nc.sbuf_tensor("vst", [128, RING * 1024], BF16))
            qg = E2(nc.sbuf_tensor("qg", [128, 2 * 2048], BF16))
            xo = E2(nc.sbuf_tensor("xo", [128, 2048], F32))
            p_sb = E2(nc.sbuf_tensor("p_sb", [128, 3 * 256], BF16))
            a_sb = E2(nc.sbuf_tensor("a_sb", [128, 2048], F32))
            at_sb = E2(nc.sbuf_tensor("at_sb", [128, 2048], BF16))
            recip = E2(nc.sbuf_tensor("recip", [128, 2], F32))
            x1 = E2(nc.sbuf_tensor("x1", [128, 2048], F32))
            x1t = E2(nc.sbuf_tensor("x1t", [128, 2048], BF16))
            stT = E2(nc.sbuf_tensor("stT", [128, 512], F32))
            rep_sb = E2(nc.sbuf_tensor("rep_sb", [128, 512], F32))
            u1cs = E2(nc.sbuf_tensor("u1cs", [128, 2], F32))
            p1_sb = E2(nc.sbuf_tensor("p1_sb", [128, 512], BF16))
            h_sb = E2(nc.sbuf_tensor("h_sb", [128, 2048], BF16))
            q2_sb = E2(nc.sbuf_tensor("q2_sb", [128, 512], BF16))
            outb = E2(nc.sbuf_tensor("outb", [128, 2048], F32))
            st1 = E2(nc.sbuf_tensor("st1", [128, 24], F32))

            blk2 = E2(nc.Block("p2"))

            DK_FIN = [sum(16 * len([j for j in range(4) if 4 * c + j >= R])
                          for c in range(3, 16) if c % 2 == p) for p in range(2)]
            DV_FIN = [16 * len([k for k in range(R, 64) if k % 2 == p]) for p in range(2)]
            DQ_FIN = [16 * len([c for c in range(16) if c % 2 == p]) for p in range(2)]

            def p1_dma_fence(eng):
                for p in range(2):
                    eng.wait_ge(dk2[p], DK_FIN[p])
                    eng.wait_ge(dv2[p], DV_FIN[p])
                    eng.wait_ge(dq2[p], DQ_FIN[p])

            _hbanks = [ps_a[0][:, 0:256], ps_a[0][:, 512:768],
                       ps_a[1][:, 0:256], ps_a[1][:, 512:768],
                       ps_s[0][:, 0:256], ps_s[1][:, 0:256]]

            def hbank(fc):
                return _hbanks[fc % 6]

            def sbank(kb):
                if kb == 0:
                    return ps_a[0][:, 0:256]
                if kb == 1:
                    return ps_a[0][:, 512:768]
                return ps_s[kb % 2][:, 0:256]

            @blk2.gpsimd
            def _(g):
                p1_dma_fence(g)
                for vc in range(8):
                    g.dma_start(out=wo_sb[:, vc * 1024:(vc + 1) * 1024],
                                in_=wo_d.ap()[vc * 128:(vc + 1) * 128, :]).then_inc(sem_w2, 16)
                for rc in range(2):
                    g.dma_start(out=v1_sb[:, rc * 4096:(rc + 1) * 4096],
                                in_=v1_d.ap()[rc * 128:(rc + 1) * 128, :]).then_inc(sem_w3, 16)
                for fc in range(32):
                    g.dma_start(out=u2_sb[:, fc * 256:(fc + 1) * 256],
                                in_=u2_d.ap()[fc * 128:(fc + 1) * 128, :]).then_inc(sem_w3, 16)
                for g_i in range(NG):
                    for kb in range(R, 4 * g_i + 4):
                        si = SI[(g_i, kb)]
                        if si >= RING:
                            g.wait_ge(sem_av, av_idx(*STREAM[si - RING]))
                        slot = si % RING
                        g.dma_start(out=vst[:, slot * 1024:(slot + 1) * 1024],
                                    in_=vscr_a[kb - R]).then_inc(vstS[slot], 16)
                    for rel in range(4):
                        kb = 4 * g_i + rel
                        it = n_av_before(g_i) + kb
                        g.wait_ge(sem_exp, it + 1)
                        g.tensor_tensor(p_sb[:, (it % 3) * 256:(it % 3 + 1) * 256],
                                        p_sb[:, (it % 3) * 256:(it % 3 + 1) * 256],
                                        msk_sb[:, rel * 256:(rel + 1) * 256],
                                        ALU.mult).then_inc(sem_msk, 1)

            @blk2.sync
            def _(s):
                p1_dma_fence(s)
                for gg in range(2):
                    s.dma_start(out=qg[:, (gg % 2) * 2048:((gg % 2) + 1) * 2048],
                                in_=qscr_a[gg]).then_inc(qsem[gg % 2], 16)
                for st_i in range(2):
                    s.dma_start(out=xo[:, st_i * 1024:(st_i + 1) * 1024],
                                in_=x_own_a[st_i * 128:(st_i + 1) * 128, :]
                                ).then_inc(sem_xo, 16)
                def out_dma(g2):
                    for st_i in range(2):
                        ti = 2 * g2 + st_i
                        s.wait_ge(sem_ln2, ti + 1)
                        s.dma_start(out=out_a[ti * 128:(ti + 1) * 128, :],
                                    in_=outb[:, (ti % 2) * 1024:((ti % 2) + 1) * 1024]
                                    ).then_inc(outd[ti % 2], 16)

                for g_i in range(NG):
                    if g_i > 0:
                        out_dma(g_i - 1)
                    for kb in range(R, 4 * g_i + 4):
                        si = SI[(g_i, kb)]
                        if si >= RING:
                            s.wait_ge(sem_av, av_idx(*STREAM[si - RING]))
                        slot = si % RING
                        s.dma_start(out=kst[:, slot * 1024:(slot + 1) * 1024],
                                    in_=kscr_a[kb - R]).then_inc(kstS[slot], 16)
                    gg = g_i + 2
                    if gg < NG:
                        s.wait_ge(sem_score, n_av_before(g_i + 1))
                        s.dma_start(out=qg[:, (gg % 2) * 2048:((gg % 2) + 1) * 2048],
                                    in_=qscr_a[gg]).then_inc(qsem[gg % 2], 16)
                    gx = g_i + 1
                    if gx < NG:
                        s.wait_ge(sem_r1h, 4 * (g_i + 1))
                        for st_i in range(2):
                            s.dma_start(out=xo[:, st_i * 1024:(st_i + 1) * 1024],
                                        in_=x_own_a[(2 * gx + st_i) * 128:(2 * gx + st_i + 1) * 128, :]
                                        ).then_inc(sem_xo, 16)
                out_dma(NG - 1)
                s.wait_ge(outd[0], 16 * 16)
                s.wait_ge(outd[1], 16 * 16)

            @blk2.tensor
            def _(t):
                p1_dma_fence(t)
                t.wait_ge(sem_w, 16 * 14)
                for rc in range(2):
                    for dc in range(8):
                        mm = t.matmul(ps_s[rc][:, 0:1],
                                      u1_sb[:, dc * 256 + rc * 128:dc * 256 + (rc + 1) * 128],
                                      ones_b[:, :],
                                      start=(dc == 0), stop=(dc == 7))
                    mm.then_inc(sem_u1c, 1)
                t.wait_ge(sem_u1ce, 2)

                def k_tile(g_i, kb, dc):
                    if kb < R:
                        return kres[:, dc * (R * 128) + kb * 128: dc * (R * 128) + (kb + 1) * 128]
                    slot = SI[(g_i, kb)] % RING
                    return kst[:, slot * 1024 + dc * 128: slot * 1024 + (dc + 1) * 128]

                def v_half(g_i, kb, h):
                    if kb < R:
                        return vres[:, kb * 1024 + h * 512: kb * 1024 + (h + 1) * 512]
                    slot = SI[(g_i, kb)] % RING
                    return vst[:, slot * 1024 + h * 512: slot * 1024 + (h + 1) * 512]

                def emit_scores(g_i, kb):
                    it = n_av_before(g_i) + kb
                    qb = (g_i % 2) * 2048
                    if kb == 0:
                        t.wait_ge(qsem[g_i % 2], 16 * (g_i // 2 + 1))
                        t.wait_ge(sem_gelu, 32 * g_i)
                    if kb == 2 and g_i > 0:
                        t.wait_ge(sem_r2h, 4 * g_i - 1)
                    if kb == 3 and g_i > 0:
                        t.wait_ge(sem_r2h, 4 * g_i)
                    if it >= 2:
                        t.wait_ge(sem_exp, it - 1)
                    if kb >= R:
                        si = SI[(g_i, kb)]
                        t.wait_ge(kstS[si % RING], 16 * (si // RING + 1))
                    for dc in range(8):
                        mm = t.matmul(sbank(kb),
                                      k_tile(g_i, kb, dc),
                                      qg[:, qb + dc * 256:qb + (dc + 1) * 256],
                                      start=(dc == 0), stop=(dc == 7))
                    mm.then_inc(sem_score, 1)

                def emit_av(g_i, kb):
                    it = n_av_before(g_i) + kb
                    if kb >= 4 * g_i:
                        t.wait_ge(sem_msk, 4 * g_i + (kb - 4 * g_i) + 1)
                    else:
                        t.wait_ge(sem_exp, it + 1)
                    if kb == 0:
                        t.wait_ge(sem_exp, it + 2)
                        t.wait_ge(sem_ge, 2 * g_i)
                        t.wait_ge(sem_gae, 8 * g_i)
                        t.wait_ge(sem_q2e, 2 * g_i)
                    if kb >= R:
                        si = SI[(g_i, kb)]
                        t.wait_ge(vstS[si % RING], 16 * (si // RING + 1))
                    pb = (it % 3) * 256
                    for st_i in range(2):
                        pt = p_sb[:, pb + st_i * 128:pb + (st_i + 1) * 128]
                        t.matmul(ps_x[st_i][:, 0:1], pt, ones_b[:, :],
                                 start=(kb == 0), stop=(kb == 4 * g_i + 3))
                        t.matmul(ps_a[st_i][:, 0:512], pt, v_half(g_i, kb, 0),
                                 start=(kb == 0), stop=(kb == 4 * g_i + 3))
                        mm = t.matmul(ps_a[st_i][:, 512:1024], pt, v_half(g_i, kb, 1),
                                      start=(kb == 0), stop=(kb == 4 * g_i + 3))
                    mm.then_inc(sem_av, 1)

                for g_i in range(NG):
                    nkb = 4 * g_i + 4
                    if g_i == 0:
                        emit_scores(0, 0)
                        emit_scores(0, 1)
                    for kb in range(nkb):
                        emit_av(g_i, kb)
                        if kb + 2 < nkb:
                            emit_scores(g_i, kb + 2)
                    for i in range(16):
                        st_i, vc = divmod(i, 8)
                        if i % 2 == 0:
                            t.wait_ge(sem_gae, 8 * g_i + i // 2 + 1)
                        if i == 2:
                            t.wait_ge(sem_ge, 2 * g_i + 1)
                        if i == 3:
                            t.wait_ge(sem_ge, 2 * g_i + 2)
                        if 16 * g_i + i >= 4:
                            t.wait_ge(sem_trpe, 16 * g_i + i - 3)
                        t.transpose((ps_s + ps_x)[i % 4][:, 0:128],
                                    a_sb[:, st_i * 1024 + vc * 128:st_i * 1024 + (vc + 1) * 128],
                                    idf[:, :]).then_inc(sem_trp, 1)
                    if g_i == 0:
                        t.wait_ge(sem_w2, 16 * 8)
                    for st_i in range(2):
                        if st_i == 0:
                            t.wait_ge(sem_gae, 8 * g_i + 4)
                        for h in range(2):
                            wu = 4 * g_i + st_i * 2 + h
                            wo_out = (ps_a[0][:, h * 512:(h + 1) * 512] if st_i == 0
                                      else ps_s[h][:, 0:512])
                            if wu >= 2:
                                t.wait_ge(sem_r1h, wu - 1)
                            for vc in range(8):
                                t.wait_ge(sem_trpe, 16 * g_i + st_i * 8 + vc + 1)
                                mm = t.matmul(wo_out,
                                              at_sb[:, vc * 256 + st_i * 128:vc * 256 + (st_i + 1) * 128],
                                              wo_sb[:, vc * 1024 + h * 512:vc * 1024 + (h + 1) * 512],
                                              start=(vc == 0), stop=(vc == 7))
                            mm.then_inc(sem_wo, 1)
                    for st_i in range(2):
                        for dc in range(8):
                            i = st_i * 8 + dc
                            if i == 0:
                                t.wait_ge(sem_r1h, 4 * g_i + 2)
                            if i == 2:
                                t.wait_ge(sem_r1h, 4 * g_i + 3)
                            if i == 3:
                                t.wait_ge(sem_r1h, 4 * (g_i + 1))
                            if 16 * g_i + i >= 4:
                                t.wait_ge(sem_xte, 16 * g_i + i - 3)
                            t.transpose((ps_x + ps_s)[i % 4][:, 0:128],
                                        x1[:, st_i * 1024 + dc * 128:st_i * 1024 + (dc + 1) * 128],
                                        idf[:, :]).then_inc(sem_xt, 1)
                    for rc in range(2):
                        t.wait_ge(sem_xte, 16 * g_i + 13 + rc)
                        if 2 * g_i + rc >= 2:
                            t.wait_ge(sem_p1e, 2 * g_i + rc - 1)
                        for dc in range(8):
                            t.wait_ge(sem_xte, 16 * g_i + min(9 + dc, 16))
                            mm = t.matmul(ps_x[rc][:, 0:256],
                                          u1_sb[:, dc * 256 + rc * 128:dc * 256 + (rc + 1) * 128],
                                          x1t[:, dc * 256:(dc + 1) * 256],
                                          start=(dc == 0), stop=(dc == 7))
                        mm.then_inc(sem_p1, 1)
                    t.wait_ge(sem_st1, g_i + 1)
                    for st_i in range(2):
                        bank = ps_a[1 - st_i]
                        t.transpose(bank[0:1, 0:128],
                                    st1[:, st_i * 12 + 7:st_i * 12 + 8],
                                    idf[:, :]).then_inc(sem_stt, 1)
                        t.transpose(bank[0:1, 512:640],
                                    st1[:, st_i * 12 + 8:st_i * 12 + 9],
                                    idf[:, :]).then_inc(sem_stt, 1)
                    t.wait_ge(sem_ste, 2 * (g_i + 1))
                    t.matmul(ps_a[1][:, 0:256], ones_r[0:1, 0:128],
                             stT[0:1, 0:256], start=True, stop=True).then_inc(sem_rep, 1)
                    t.matmul(ps_a[1][:, 512:768], ones_r[0:1, 0:128],
                             stT[0:1, 256:512], start=True, stop=True).then_inc(sem_rep, 1)
                    def q2_partial(fc):
                        t.wait_ge(sem_gelu, 32 * g_i + fc + 1)
                        if fc == 0 and g_i > 0:
                            t.wait_ge(sem_q2e, 2 * g_i)
                        for rc in range(2):
                            mm = t.matmul(ps_x[rc][:, 0:256],
                                          u2_sb[:, fc * 256 + rc * 128:fc * 256 + (rc + 1) * 128],
                                          h_sb[:, (fc % 8) * 256:(fc % 8 + 1) * 256],
                                          start=(fc == 0), stop=(fc == 31))
                        mm.then_inc(sem_q2p, 1)

                    if g_i == 0:
                        t.wait_ge(sem_w3, 16 * 34)
                    t.wait_ge(sem_p1e, 2 * (g_i + 1))
                    for fc in range(32):
                        if fc >= 6:
                            t.wait_ge(sem_gelu, 32 * g_i + fc - 5)
                        for rc in range(2):
                            mm = t.matmul(hbank(fc),
                                          v1_sb[:, rc * 4096 + fc * 128:rc * 4096 + (fc + 1) * 128],
                                          p1_sb[:, rc * 256:(rc + 1) * 256],
                                          start=(rc == 0), stop=(rc == 1))
                        mm.then_inc(sem_h, 1)
                        if fc >= 4:
                            q2_partial(fc - 4)
                    for fc in range(28, 32):
                        q2_partial(fc)
                    if g_i + 1 < NG:
                        emit_scores(g_i + 1, 0)
                        emit_scores(g_i + 1, 1)
                    t.wait_ge(sem_q2e, 2 * (g_i + 1))
                    for st_i in range(2):
                        for h in range(2):
                            fu = 4 * g_i + st_i * 2 + h
                            if fu >= 2:
                                t.wait_ge(sem_r2h, fu - 1)
                            for rc in range(2):
                                mm = t.matmul(ps_s[fu % 2][:, 0:512],
                                              q2_sb[:, rc * 256 + st_i * 128:rc * 256 + (st_i + 1) * 128],
                                              v2_sb[:, rc * 1024 + h * 512:rc * 1024 + (h + 1) * 512],
                                              start=(rc == 0), stop=(rc == 1))
                            mm.then_inc(sem_f, 1)

            @blk2.scalar
            def _(sc):
                p1_dma_fence(sc)
                def ln_sq(tile_ap, scratch_ap, ssq_ap):
                    sc.activation(scratch_ap, tile_ap, AF.Square, accum_out=ssq_ap)

                def emit_exp(g_i, kb):
                    it = n_av_before(g_i) + kb
                    sc.wait_ge(sem_score, it + 1)
                    if it >= 3:
                        sc.wait_ge(sem_av, it - 2)
                    sc.activation(p_sb[:, (it % 3) * 256:(it % 3 + 1) * 256],
                                  sbank(kb), AF.Exp,
                                  bias=shift_s[:, 0:1], scale=SCALE).then_inc(sem_exp, 1)

                def sqrt2_pair(g2):
                    sc.wait_ge(sem_vb, g2 + 1)
                    sc.activation(bass.AP(st1, 6, [[24, 128], [12, 2]]),
                                  bass.AP(st1, 5, [[24, 128], [12, 2]]),
                                  AF.Sqrt, bias=eps_s[:, 0:1]).then_inc(sem_sd2, 1)

                for g_i in range(NG):
                    nkb = 4 * g_i + 4
                    if g_i == 0:
                        for kb in range(4):
                            emit_exp(0, kb)
                    for idx, kb in enumerate(range(6, nkb)):
                        emit_exp(g_i, kb)
                        if idx == 1 and g_i >= 1:
                            sqrt2_pair(g_i - 1)
                    sc.wait_ge(sem_av, n_av_before(g_i + 1))
                    if g_i > 0:
                        sc.wait_ge(sem_trp, 16 * g_i)
                    for st_i in range(2):
                        for p in range(4):
                            sc.activation(a_sb[:, st_i * 1024 + p * 256:st_i * 1024 + (p + 1) * 256],
                                          ps_a[st_i][:, p * 256:(p + 1) * 256],
                                          AF.Copy).then_inc(sem_gae, 1)
                    sc.drain()
                    sc.wait_ge(sem_q2p, 32 * g_i)
                    for st_i in range(2):
                        sc.wait_ge(sem_r1h, 4 * g_i + 2 * (st_i + 1))
                        ln_sq(x1[:, st_i * 1024:(st_i + 1) * 1024],
                              h_sb[:, st_i * 1024:(st_i + 1) * 1024],
                              st1[:, 1 + st_i * 12:2 + st_i * 12])
                        sc.drain().then_inc(sem_sq1, 1)
                    sc.wait_ge(sem_va, g_i + 1)
                    sc.activation(bass.AP(st1, 6, [[24, 128], [12, 2]]),
                                  bass.AP(st1, 5, [[24, 128], [12, 2]]), AF.Sqrt,
                                  bias=eps_s[:, 0:1]).then_inc(sem_sd, 1)
                    for fc in range(32):
                        sc.wait_ge(sem_h, 32 * g_i + fc + 1)
                        if fc >= 8:
                            sc.wait_ge(sem_q2p, 32 * g_i + fc - 7)
                        sc.activation(h_sb[:, (fc % 8) * 256:(fc % 8 + 1) * 256],
                                      hbank(fc), AF.Gelu_apprx_tanh,
                                      bias=b1_sb[:, fc:fc + 1]).then_inc(sem_gelu, 1)
                    if g_i > 0:
                        sc.wait_ge(sem_f, 4 * g_i)
                    sc.wait_ge(sem_q2p, 32 * (g_i + 1))
                    for rc in range(2):
                        sc.activation(q2_sb[:, rc * 256:(rc + 1) * 256],
                                      ps_x[rc][:, 0:256],
                                      AF.Copy).then_inc(sem_q2e, 1)
                    if g_i + 1 < NG:
                        for kb in range(6):
                            emit_exp(g_i + 1, kb)
                sqrt2_pair(NG - 1)

            @blk2.vector
            def _(v):
                p1_dma_fence(v)
                for rc in range(2):
                    v.wait_ge(sem_u1c, rc + 1)
                    v.tensor_copy(u1cs[:, rc:rc + 1],
                                  ps_s[rc][:, 0:1]).then_inc(sem_u1ce, 1)

                def stp(c):
                    return bass.AP(st1, c, [[24, 128], [12, 2]])

                def ln_tail_paired(sem_va_, sem_sd_, need, sem_st_):
                    v.tensor_scalar(stp(2), stp(0), -1.0 / 1024.0, None, ALU.mult)
                    v.tensor_scalar(stp(3), stp(1), 1.0 / 1024.0, None, ALU.mult)
                    v.drain()
                    v.tensor_tensor(stp(4), stp(2), stp(2), ALU.mult)
                    v.drain()
                    v.tensor_tensor(stp(5), stp(3), stp(4), ALU.subtract)
                    v.drain().then_inc(sem_va_, 1)
                    v.wait_ge(sem_sd_, need)
                    v.reciprocal(stp(7), stp(6))
                    v.drain()
                    v.tensor_tensor(stp(8), stp(2), stp(7), ALU.mult)
                    v.drain().then_inc(sem_st_, 1)

                for g_i in range(NG):
                    nkb = 4 * g_i + 4
                    v.wait_ge(sem_av, n_av_before(g_i + 1))
                    for st_i in range(2):
                        v.reciprocal(recip[:, st_i:st_i + 1],
                                     ps_x[st_i][:, 0:1]).then_inc(sem_ge, 1)
                    if g_i > 0:
                        v.wait_ge(sem_wo, 4 * g_i)
                    for i in range(16):
                        st_i, vc = divmod(i, 8)
                        v.wait_ge(sem_trp, 16 * g_i + i + 1)
                        v.tensor_copy(at_sb[:, vc * 256 + st_i * 128:vc * 256 + (st_i + 1) * 128],
                                      (ps_s + ps_x)[i % 4][:, 0:128]).then_inc(sem_trpe, 1)
                    v.drain()
                    if g_i > 0:
                        v.wait_ge(sem_ln2, 2 * g_i)
                        v.wait_ge(sem_sq2, 2 * g_i)
                    v.wait_ge(sem_xo, 32 * (g_i + 1))
                    for st_i in range(2):
                        for h in range(2):
                            wu = 4 * g_i + st_i * 2 + h
                            v.wait_ge(sem_wo, wu + 1)
                            wo_src = (ps_a[0][:, h * 512:(h + 1) * 512] if st_i == 0
                                      else ps_s[h][:, 0:512])
                            v.scalar_tensor_tensor(
                                x1[:, st_i * 1024 + h * 512:st_i * 1024 + (h + 1) * 512],
                                wo_src,
                                recip[:, st_i:st_i + 1],
                                xo[:, st_i * 1024 + h * 512:st_i * 1024 + (h + 1) * 512],
                                ALU.mult, ALU.add).then_inc(sem_r1h, 1)
                    v.drain()
                    if g_i > 0:
                        v.wait_ge(sem_p1, 2 * g_i)
                    for i in range(16):
                        st_i, dc = divmod(i, 8)
                        v.wait_ge(sem_xt, 16 * g_i + i + 1)
                        v.tensor_copy(x1t[:, dc * 256 + st_i * 128:dc * 256 + (st_i + 1) * 128],
                                      (ps_x + ps_s)[i % 4][:, 0:128]).then_inc(sem_xte, 1)
                    for st_i in range(2):
                        o = st_i * 12
                        v.wait_ge(sem_sq1, 2 * g_i + st_i + 1)
                        v.reduce_sum(st1[:, 0 + o:1 + o],
                                     x1[:, st_i * 1024:(st_i + 1) * 1024],
                                     axis=mybir.AxisListType.X)
                    v.drain()
                    ln_tail_paired(sem_va, sem_sd, g_i + 1, sem_st1)
                    for st_i in range(2):
                        v.wait_ge(sem_stt, 4 * g_i + 2 * (st_i + 1))
                        bank = ps_a[1 - st_i]
                        v.tensor_copy(stT[0:1, st_i * 128:(st_i + 1) * 128],
                                      bank[0:1, 0:128])
                        v.tensor_copy(stT[0:1, 256 + st_i * 128:256 + (st_i + 1) * 128],
                                      bank[0:1, 512:640]).then_inc(sem_ste, 1)
                    v.drain()
                    for st_i in range(2):
                        o = st_i * 12
                        v.wait_ge(sem_xt, 16 * g_i + 8 * (st_i + 1))
                        v.tensor_scalar(x1[:, st_i * 1024:(st_i + 1) * 1024],
                                        x1[:, st_i * 1024:(st_i + 1) * 1024],
                                        st1[:, 2 + o:3 + o], st1[:, 7 + o:8 + o],
                                        ALU.add, ALU.mult)
                        v.drain()
                        v.tensor_tensor(x1[:, st_i * 1024:(st_i + 1) * 1024],
                                        x1[:, st_i * 1024:(st_i + 1) * 1024],
                                        b2_sb[:, :], ALU.add)
                    v.drain()
                    v.wait_ge(sem_rep, 2 * (g_i + 1))
                    v.tensor_copy(rep_sb[:, 0:256], ps_a[1][:, 0:256])
                    v.tensor_copy(rep_sb[:, 256:512], ps_a[1][:, 512:768])
                    v.drain()
                    if g_i > 0:
                        v.wait_ge(sem_h, 32 * g_i)
                    for rc in range(2):
                        v.wait_ge(sem_p1, 2 * g_i + rc + 1)
                        v.tensor_tensor(a_sb[:, rc * 256:(rc + 1) * 256],
                                        ps_x[rc][:, 0:256],
                                        rep_sb[:, 0:256], ALU.mult)
                        v.drain()
                        v.scalar_tensor_tensor(p1_sb[:, rc * 256:(rc + 1) * 256],
                                               rep_sb[:, 256:512],
                                               u1cs[:, rc:rc + 1],
                                               a_sb[:, rc * 256:(rc + 1) * 256],
                                               ALU.mult, ALU.add).then_inc(sem_p1e, 1)
                    for st_i in range(2):
                        for h in range(2):
                            fu = 4 * g_i + st_i * 2 + h
                            v.wait_ge(sem_f, fu + 1)
                            v.tensor_tensor(x1[:, st_i * 1024 + h * 512:st_i * 1024 + (h + 1) * 512],
                                            ps_s[fu % 2][:, 0:512],
                                            x1[:, st_i * 1024 + h * 512:st_i * 1024 + (h + 1) * 512],
                                            ALU.add).then_inc(sem_r2h, 1)
                    v.drain()
                    for st_i in range(2):
                        o = st_i * 12
                        v.tensor_tensor(a_sb[:, st_i * 1024:(st_i + 1) * 1024],
                                        x1[:, st_i * 1024:(st_i + 1) * 1024],
                                        x1[:, st_i * 1024:(st_i + 1) * 1024],
                                        ALU.mult)
                        v.drain()
                        v.reduce_sum(st1[:, 1 + o:2 + o],
                                     a_sb[:, st_i * 1024:(st_i + 1) * 1024],
                                     axis=mybir.AxisListType.X)
                        v.drain().then_inc(sem_sq2, 1)
                    for st_i in range(2):
                        o = st_i * 12
                        v.reduce_sum(st1[:, 0 + o:1 + o],
                                     x1[:, st_i * 1024:(st_i + 1) * 1024],
                                     axis=mybir.AxisListType.X)
                    v.drain()
                    ln_tail_paired(sem_vb, sem_sd2, g_i + 1, sem_st2)
                    for st_i in range(2):
                        o = st_i * 12
                        ti = 2 * g_i + st_i
                        if ti >= 2:
                            v.wait_ge(outd[ti % 2], 16 * (ti // 2))
                        v.tensor_scalar(outb[:, (ti % 2) * 1024:((ti % 2) + 1) * 1024],
                                        x1[:, st_i * 1024:(st_i + 1) * 1024],
                                        st1[:, 2 + o:3 + o], st1[:, 7 + o:8 + o],
                                        ALU.add, ALU.mult)
                        v.drain().then_inc(sem_ln2, 1)

    return nc


_NC_CACHE = {}


def kernel(L0_attn_x, L0_attn_Wq, L0_attn_Wk, L0_attn_Wv, L0_attn_Wo,
           L0_ffn_U1, L0_ffn_V1, L0_ffn_b1, L0_ffn_U2, L0_ffn_V2, L0_ffn_b2):
    x = np.ascontiguousarray(np.asarray(L0_attn_x, dtype=np.float32))
    B, Sq, D = x.shape
    assert (B, Sq, D) == (4, 8192, 1024)

    wq = np.ascontiguousarray(np.asarray(L0_attn_Wq, np.float32))
    wk = np.ascontiguousarray(np.asarray(L0_attn_Wk, np.float32))
    wv = np.ascontiguousarray(np.asarray(L0_attn_Wv, np.float32))
    wo = np.ascontiguousarray(np.asarray(L0_attn_Wo, np.float32))
    u1 = np.ascontiguousarray(np.asarray(L0_ffn_U1, np.float32))
    v1 = np.ascontiguousarray(np.asarray(L0_ffn_V1, np.float32))
    b1 = np.asarray(L0_ffn_b1, np.float32)
    u2 = np.ascontiguousarray(np.asarray(L0_ffn_U2, np.float32))
    v2 = np.ascontiguousarray(np.asarray(L0_ffn_V2, np.float32))
    b2 = np.asarray(L0_ffn_b2, np.float32)

    b1t = np.ascontiguousarray(b1.reshape(32, 128).T)
    b2b = np.ascontiguousarray(np.broadcast_to(b2, (128, 1024)))
    ident = np.eye(128, dtype=np.float32)

    tri = (np.arange(128)[:, None] <= np.arange(128)[None, :]).astype(np.float32)
    full = np.ones((128, 128), np.float32)
    zero = np.zeros((128, 128), np.float32)

    def mask_for(par):
        m = np.zeros((128, 1024), np.float32)
        for rel in range(4):
            for s in range(2):
                d = 2 * s + par
                blkm = full if rel < d else (tri if rel == d else zero)
                m[:, rel * 256 + s * 128: rel * 256 + (s + 1) * 128] = blkm
        return m

    masks = [mask_for(0), mask_for(1)]

    if "nc" not in _NC_CACHE:
        _NC_CACHE["nc"] = build_program()
    nc = _NC_CACHE["nc"]

    in_maps = []
    for c in range(8):
        b, par = c >> 1, c & 1
        xb = x[b]
        x_own = np.ascontiguousarray(
            xb.reshape(32, 2, 128, 1024)[:, par].reshape(4096, 1024))
        in_maps.append({
            "x_seq": xb, "x_own": x_own,
            "wq": wq, "wk": wk, "wv": wv, "wo": wo,
            "u1": u1, "v1": v1, "b1t": b1t, "u2": u2, "v2": v2, "b2b": b2b,
            "masks": masks[par], "ident": ident,
        })

    res = run_bass_kernel_spmd(nc, in_maps, core_ids=list(range(8)))

    out = np.empty((4, 8192, 1024), np.float32)
    for c in range(8):
        b, par = c >> 1, c & 1
        oc = res.results[c]["out"]
        out[b].reshape(32, 2, 128, 1024)[:, par] = oc.reshape(32, 128, 1024)
    return out


# version 49
# speedup vs baseline: 1.0120x; 1.0120x over previous
import numpy as np
import contextlib
import concourse.bass as bass
import concourse.mybir as mybir
from concourse.bass_utils import run_bass_kernel_spmd

F32 = mybir.dt.float32
BF16 = mybir.dt.bfloat16
AF = mybir.ActivationFunctionType
ALU = mybir.AluOpType

SHIFT = 12.0
SCALE = 1.0 / 32.0
R = 13
RING = 8
NG = 16
NCHUNK = 16
EPS = 1e-5

STREAM = [(g, kb) for g in range(NG) for kb in range(R, 4 * g + 4)]
SI = {gk: i for i, gk in enumerate(STREAM)}
NSTREAM_BLOCKS = 64 - R


def n_av_before(g):
    return 2 * g * g + 2 * g


def av_idx(g, kb):
    return n_av_before(g) + kb + 1


def build_program():
    nc = bass.Bass("TRN2", target_bir_lowering=False)

    x_seq = nc.dram_tensor("x_seq", [8192, 1024], F32, kind="ExternalInput")
    x_own = nc.dram_tensor("x_own", [4096, 1024], F32, kind="ExternalInput")
    wq_d = nc.dram_tensor("wq", [1024, 1024], F32, kind="ExternalInput")
    wk_d = nc.dram_tensor("wk", [1024, 1024], F32, kind="ExternalInput")
    wv_d = nc.dram_tensor("wv", [1024, 1024], F32, kind="ExternalInput")
    wo_d = nc.dram_tensor("wo", [1024, 1024], F32, kind="ExternalInput")
    u1_d = nc.dram_tensor("u1", [1024, 256], F32, kind="ExternalInput")
    v1_d = nc.dram_tensor("v1", [256, 4096], F32, kind="ExternalInput")
    b1_d = nc.dram_tensor("b1t", [128, 32], F32, kind="ExternalInput")
    u2_d = nc.dram_tensor("u2", [4096, 256], F32, kind="ExternalInput")
    v2_d = nc.dram_tensor("v2", [256, 1024], F32, kind="ExternalInput")
    b2_d = nc.dram_tensor("b2b", [128, 1024], F32, kind="ExternalInput")
    msk_d = nc.dram_tensor("masks", [128, 1024], F32, kind="ExternalInput")
    id_d = nc.dram_tensor("ident", [128, 128], F32, kind="ExternalInput")
    out_d = nc.dram_tensor("out", [4096, 1024], F32, kind="ExternalOutput")

    xbf = nc.dram_tensor("xbf", [8192, 1024], BF16, kind="Internal")
    xobf = nc.dram_tensor("xobf", [4096, 1024], BF16, kind="Internal")
    qscr = nc.dram_tensor("qscr", [NG, 128, 2048], BF16, kind="Internal")
    kscr = nc.dram_tensor("kscr", [NSTREAM_BLOCKS, 128, 1024], BF16, kind="Internal")
    vscr = nc.dram_tensor("vscr", [NSTREAM_BLOCKS, 128, 1024], BF16, kind="Internal")

    x_seq_a, x_own_a = x_seq.ap(), x_own.ap()
    out_a = out_d.ap()
    xbf_a, xobf_a = xbf.ap(), xobf.ap()
    qscr_a, kscr_a, vscr_a = qscr.ap(), kscr.ap(), vscr.ap()

    ctx = contextlib.ExitStack()
    with ctx:
        E = ctx.enter_context
        idf = E(nc.sbuf_tensor("idf", [128, 128], F32))
        ones_b = E(nc.sbuf_tensor("ones_b", [128, 1], BF16))
        ones_r = E(nc.sbuf_tensor("ones_r", [128, 128], F32))
        shift_s = E(nc.sbuf_tensor("shift_s", [128, 1], F32))
        eps_s = E(nc.sbuf_tensor("eps_s", [128, 1], F32))
        msk_sb = E(nc.sbuf_tensor("msk_sb", [128, 1024], BF16))
        b1_sb = E(nc.sbuf_tensor("b1_sb", [128, 32], F32))
        b2_sb = E(nc.sbuf_tensor("b2_sb", [128, 1024], F32))
        u1_sb = E(nc.sbuf_tensor("u1_sb", [128, 2048], BF16))
        v2_sb = E(nc.sbuf_tensor("v2_sb", [128, 2048], BF16))
        kres = E(nc.sbuf_tensor("kres", [128, 8 * R * 128], BF16))
        vres = E(nc.sbuf_tensor("vres", [128, R * 1024], BF16))

        ps_s = [E(nc.psum_tensor(f"ps_s{i}", [128, 512], F32)) for i in range(2)]
        ps_a = [E(nc.psum_tensor(f"ps_a{i}", [128, 1024], F32)) for i in range(2)]
        ps_x = [E(nc.psum_tensor(f"ps_x{i}", [128, 512], F32)) for i in range(2)]

        S = lambda name: E(nc.semaphore(name))
        sem_w = S("sem_w")
        sem_wk1 = S("sem_wk1")
        sem_wv1 = S("sem_wv1")
        sem_wq1 = S("sem_wq1")
        sem_ms = S("sem_ms")
        gc = [S(f"gc{i}") for i in range(4)]
        str2 = [S(f"str{i}") for i in range(2)]
        str_e = [[S(f"se{p}_{e}") for e in range(8)] for p in range(2)]
        t_k, t_v, t_q = S("t_k"), S("t_v"), S("t_q")
        v_k, v_v, v_q = S("v_k"), S("v_v"), S("v_q")
        dk2 = [S(f"dk{i}") for i in range(2)]
        dv2 = [S(f"dv{i}") for i in range(2)]
        dq2 = [S(f"dq{i}") for i in range(2)]

        sem_w2 = S("sem_w2")
        sem_w3 = S("sem_w3")
        qsem = [S(f"qsem{i}") for i in range(2)]
        sem_xo = S("sem_xo")
        kstS = [S(f"kstS{i}") for i in range(RING)]
        vstS = [S(f"vstS{i}") for i in range(RING)]
        sem_score = S("sem_score")
        sem_exp = S("sem_exp")
        sem_msk = S("sem_msk")
        sem_av = S("sem_av")
        sem_ge = S("sem_ge")
        sem_gae = S("sem_gae")
        sem_q2p = S("sem_q2p")
        sem_trp = S("sem_trp")
        sem_trpe = S("sem_trpe")
        sem_trpp = S("sem_trpp")
        sem_wo = S("sem_wo")
        sem_r1h = S("sem_r1h")
        sem_sq1 = S("sem_sq1")
        sem_va = S("sem_va")
        sem_sd = S("sem_sd")
        sem_st1 = S("sem_st1")
        sem_stt = S("sem_stt")
        sem_ste = S("sem_ste")
        sem_rep = S("sem_rep")
        sem_u1c = S("sem_u1c")
        sem_u1ce = S("sem_u1ce")
        sem_xt = S("sem_xt")
        sem_xte = S("sem_xte")
        sem_p1 = S("sem_p1")
        sem_p1e = S("sem_p1e")
        sem_h = S("sem_h")
        sem_gelu = S("sem_gelu")
        sem_q2 = S("sem_q2")
        sem_q2e = S("sem_q2e")
        sem_f = S("sem_f")
        sem_r2h = S("sem_r2h")
        sem_sq2 = S("sem_sq2")
        sem_vb = S("sem_vb")
        sem_sd2 = S("sem_sd2")
        sem_st2 = S("sem_st2")
        sem_ln2 = S("sem_ln2")
        outd = [S(f"outd{i}") for i in range(2)]

        p1ctx = contextlib.ExitStack()
        with p1ctx:
            E1 = p1ctx.enter_context
            wq_sb = E1(nc.sbuf_tensor("wq_sb", [128, 8192], BF16))
            wk_sb = E1(nc.sbuf_tensor("wk_sb", [128, 8192], BF16))
            wv_sb = E1(nc.sbuf_tensor("wv_sb", [128, 8192], BF16))
            xt = E1(nc.sbuf_tensor("xt", [128, 2 * 4096], BF16))
            xot = E1(nc.sbuf_tensor("xot", [128, 2 * 2048], BF16))
            kstage = E1(nc.sbuf_tensor("kstage", [128, 2 * 4096], BF16))
            vstage = E1(nc.sbuf_tensor("vstage", [128, 2 * 1024], BF16))
            qstage = E1(nc.sbuf_tensor("qstage", [128, 2 * 2048], BF16))

            blk1 = E1(nc.Block("p1"))

            @blk1.gpsimd
            def _(g):
                g.memset(ones_b[:, :], 1.0)
                g.memset(ones_r[0:1, :], 1.0)
                g.memset(shift_s[:, :], -SHIFT)
                g.memset(eps_s[:, :], EPS)
                g.drain().then_inc(sem_ms, 1)

                def cast_early(tc):
                    g.dma_start(out=xbf_a[tc * 512:(tc + 1) * 512, :],
                                in_=x_seq_a[tc * 512:(tc + 1) * 512, :]).then_inc(gc[tc % 4], 16)
                    g.dma_start(out=xobf_a[tc * 256:(tc + 1) * 256, :],
                                in_=x_own_a[tc * 256:(tc + 1) * 256, :]).then_inc(gc[tc % 4], 16)
                cast_early(0)
                cast_early(1)
                for ec in range(8):
                    g.dma_start(out=wk_sb[:, ec * 1024:(ec + 1) * 1024],
                                in_=wk_d.ap()[ec * 128:(ec + 1) * 128, :]).then_inc(sem_wk1, 16)
                for ec in range(8):
                    g.dma_start(out=wv_sb[:, ec * 1024:(ec + 1) * 1024],
                                in_=wv_d.ap()[ec * 128:(ec + 1) * 128, :]).then_inc(sem_wv1, 16)
                for ec in range(8):
                    g.dma_start(out=wq_sb[:, ec * 1024:(ec + 1) * 1024],
                                in_=wq_d.ap()[ec * 128:(ec + 1) * 128, :]).then_inc(sem_wq1, 16)
                for dc in range(8):
                    g.dma_start(out=u1_sb[:, dc * 256:(dc + 1) * 256],
                                in_=u1_d.ap()[dc * 128:(dc + 1) * 128, :]).then_inc(sem_w, 16)
                for rc in range(2):
                    g.dma_start(out=v2_sb[:, rc * 1024:(rc + 1) * 1024],
                                in_=v2_d.ap()[rc * 128:(rc + 1) * 128, :]).then_inc(sem_w, 16)
                g.dma_start(out=b1_sb[:, :], in_=b1_d.ap()[:, :]).then_inc(sem_w, 16)
                g.dma_start(out=b2_sb[:, :], in_=b2_d.ap()[:, :]).then_inc(sem_w, 16)
                g.dma_start(out=msk_sb[:, :], in_=msk_d.ap()[:, :]).then_inc(sem_w, 16)
                g.dma_start(out=idf[:, :], in_=id_d.ap()[:, :]).then_inc(sem_w, 16)
                cast = cast_early
                for tc in range(NCHUNK):
                    if tc + 2 < NCHUNK:
                        cast(tc + 2)
                    if tc >= 3:
                        g.wait_ge(v_k, 8 * (tc + 1))
                        for j in range(4):
                            kb = 4 * tc + j
                            if kb < R:
                                continue
                            off = (tc % 2) * 4096 + j * 1024
                            g.dma_start(out=kscr_a[kb - R],
                                        in_=kstage[:, off:off + 1024]).then_inc(dk2[tc % 2], 16)
                        for tt in range(4):
                            kb = 4 * tc + tt
                            if kb < R:
                                continue
                            g.wait_ge(v_v, 4 * tc + tt + 1)
                            g.dma_start(
                                out=vscr_a[kb - R],
                                in_=vstage[:, (kb % 2) * 1024:((kb % 2) + 1) * 1024],
                            ).then_inc(dv2[kb % 2], 16)
                    g.wait_ge(v_q, 8 * (tc + 1))
                    g.dma_start(out=qscr_a[tc],
                                in_=qstage[:, (tc % 2) * 2048:((tc % 2) + 1) * 2048],
                                ).then_inc(dq2[tc % 2], 16)

            N0 = 24 + 8 + 2 + 4

            @blk1.sync
            def _(s):
                for tc in range(NCHUNK):
                    s.wait_ge(gc[tc % 4], 32 * (tc // 4 + 1))
                    if tc >= 2:
                        s.wait_ge(t_v, 4 * (tc - 1))
                    for ec in range(8):
                        s.dma_start(
                            out=xt[:, (tc % 2) * 4096 + ec * 512:(tc % 2) * 4096 + (ec + 1) * 512],
                            in_=xbf_a[tc * 512:(tc + 1) * 512, ec * 128:(ec + 1) * 128],
                            transpose=True).then_inc(str_e[tc % 2][ec], 16)
                    if tc >= 2:
                        s.wait_ge(t_q, 8 * (tc - 1))
                    for ec in range(8):
                        s.dma_start(
                            out=xot[:, (tc % 2) * 2048 + ec * 256:(tc % 2) * 2048 + (ec + 1) * 256],
                            in_=xobf_a[tc * 256:(tc + 1) * 256, ec * 128:(ec + 1) * 128],
                            transpose=True).then_inc(str2[tc % 2], 16)

            @blk1.tensor
            def _(t):
                t.wait_ge(sem_ms, 1)
                t.wait_ge(sem_wk1, 16 * 8)
                for tc in range(NCHUNK):
                    xb = (tc % 2) * 4096
                    xob = (tc % 2) * 2048
                    for dco in range(8):
                        u = 8 * tc + dco
                        if u >= 2:
                            t.wait_ge(v_k, u - 1)
                        for ec in range(8):
                            if dco == 0:
                                t.wait_ge(str_e[tc % 2][ec], 16 * (tc // 2 + 1))
                            mm = t.matmul(ps_s[u % 2][:, 0:512],
                                          wk_sb[:, ec * 1024 + dco * 128:ec * 1024 + (dco + 1) * 128],
                                          xt[:, xb + ec * 512:xb + (ec + 1) * 512],
                                          start=(ec == 0), stop=(ec == 7))
                        mm.then_inc(t_k, 1)
                    if tc == 0:
                        t.wait_ge(sem_wv1, 16 * 8)
                    for tt in range(4):
                        u = 4 * tc + tt
                        if u >= 2:
                            t.wait_ge(v_v, u - 1)
                        for ec in range(8):
                            lhs = xt[:, xb + ec * 512 + tt * 128:xb + ec * 512 + (tt + 1) * 128]
                            t.matmul(ps_a[u % 2][:, 0:512], lhs, wv_sb[:, ec * 1024:ec * 1024 + 512],
                                     start=(ec == 0), stop=(ec == 7))
                            mm = t.matmul(ps_a[u % 2][:, 512:1024], lhs,
                                          wv_sb[:, ec * 1024 + 512:(ec + 1) * 1024],
                                          start=(ec == 0), stop=(ec == 7))
                        mm.then_inc(t_v, 1)
                    if tc == 0:
                        t.wait_ge(sem_wq1, 16 * 8)
                    t.wait_ge(str2[tc % 2], 128 * (tc // 2 + 1))
                    for dco in range(8):
                        u = 8 * tc + dco
                        if u >= 2:
                            t.wait_ge(v_q, u - 1)
                        for ec in range(8):
                            mm = t.matmul(ps_x[u % 2][:, 0:256],
                                          wq_sb[:, ec * 1024 + dco * 128:ec * 1024 + (dco + 1) * 128],
                                          xot[:, xob + ec * 256:xob + (ec + 1) * 256],
                                          start=(ec == 0), stop=(ec == 7))
                        mm.then_inc(t_q, 1)

            @blk1.vector
            def _(v):
                for tc in range(NCHUNK):
                    def k_dmas_of(c):
                        return len([j for j in range(4) if 4 * c + j >= R])

                    for dco in range(8):
                        v.wait_ge(t_k, 8 * tc + dco + 1)
                        if tc < 3:
                            v.tensor_copy(
                                kres[:, dco * (R * 128) + tc * 512: dco * (R * 128) + (tc + 1) * 512],
                                ps_s[(8 * tc + dco) % 2][:, 0:512]).then_inc(v_k, 1)
                        else:
                            if dco == 0 and tc >= 5:
                                nprior = sum(16 * k_dmas_of(c)
                                             for c in range(3, tc - 1) if c % 2 == tc % 2)
                                v.wait_ge(dk2[tc % 2], nprior)
                            base = (tc % 2) * 4096 + dco * 128
                            cp = None
                            for j in range(4):
                                kb = 4 * tc + j
                                if kb < R:
                                    dst = kres[:, dco * (R * 128) + kb * 128:
                                               dco * (R * 128) + (kb + 1) * 128]
                                else:
                                    dst = kstage[:, base + j * 1024:base + j * 1024 + 128]
                                cp = v.tensor_copy(
                                    dst, ps_s[(8 * tc + dco) % 2][:, j * 128:(j + 1) * 128])
                            cp.then_inc(v_k, 1)
                    for tt in range(4):
                        kb = 4 * tc + tt
                        v.wait_ge(t_v, kb + 1)
                        if kb < R:
                            dst = vres[:, kb * 1024:(kb + 1) * 1024]
                        else:
                            if kb - 2 >= R:
                                first_p = R if R % 2 == kb % 2 else R + 1
                                n_writes = (kb - 2 - first_p) // 2 + 1
                                v.wait_ge(dv2[kb % 2], 16 * n_writes)
                            dst = vstage[:, (kb % 2) * 1024:((kb % 2) + 1) * 1024]
                        v.tensor_copy(dst, ps_a[kb % 2][:, 0:1024]).then_inc(v_v, 1)
                    for dco in range(8):
                        v.wait_ge(t_q, 8 * tc + dco + 1)
                        if dco == 0 and tc >= 2:
                            v.wait_ge(dq2[tc % 2], 16 * (tc // 2))
                        v.tensor_copy(qstage[:, (tc % 2) * 2048 + dco * 256:(tc % 2) * 2048 + (dco + 1) * 256],
                                      ps_x[(8 * tc + dco) % 2][:, 0:256]).then_inc(v_q, 1)

        p2ctx = contextlib.ExitStack()
        with p2ctx:
            E2 = p2ctx.enter_context
            wo_sb = E2(nc.sbuf_tensor("wo_sb", [128, 8192], BF16))
            v1_sb = E2(nc.sbuf_tensor("v1_sb", [128, 8192], BF16))
            u2_sb = E2(nc.sbuf_tensor("u2_sb", [128, 8192], BF16))
            kst = E2(nc.sbuf_tensor("kst", [128, RING * 1024], BF16))
            vst = E2(nc.sbuf_tensor("vst", [128, RING * 1024], BF16))
            qg = E2(nc.sbuf_tensor("qg", [128, 2 * 2048], BF16))
            xo = E2(nc.sbuf_tensor("xo", [128, 2048], F32))
            p_sb = E2(nc.sbuf_tensor("p_sb", [128, 3 * 256], BF16))
            a_sb = E2(nc.sbuf_tensor("a_sb", [128, 2048], F32))
            at_sb = E2(nc.sbuf_tensor("at_sb", [128, 2048], BF16))
            recip = E2(nc.sbuf_tensor("recip", [128, 2], F32))
            x1 = E2(nc.sbuf_tensor("x1", [128, 2048], F32))
            x1t = E2(nc.sbuf_tensor("x1t", [128, 2048], BF16))
            stT = E2(nc.sbuf_tensor("stT", [128, 512], F32))
            rep_sb = E2(nc.sbuf_tensor("rep_sb", [128, 512], F32))
            u1cs = E2(nc.sbuf_tensor("u1cs", [128, 2], F32))
            p1_sb = E2(nc.sbuf_tensor("p1_sb", [128, 512], BF16))
            h_sb = E2(nc.sbuf_tensor("h_sb", [128, 2048], BF16))
            q2_sb = E2(nc.sbuf_tensor("q2_sb", [128, 512], BF16))
            outb = E2(nc.sbuf_tensor("outb", [128, 2048], F32))
            st1 = E2(nc.sbuf_tensor("st1", [128, 24], F32))

            blk2 = E2(nc.Block("p2"))

            DK_FIN = [sum(16 * len([j for j in range(4) if 4 * c + j >= R])
                          for c in range(3, 16) if c % 2 == p) for p in range(2)]
            DV_FIN = [16 * len([k for k in range(R, 64) if k % 2 == p]) for p in range(2)]
            DQ_FIN = [16 * len([c for c in range(16) if c % 2 == p]) for p in range(2)]

            def p1_dma_fence(eng):
                for p in range(2):
                    eng.wait_ge(dk2[p], DK_FIN[p])
                    eng.wait_ge(dv2[p], DV_FIN[p])
                    eng.wait_ge(dq2[p], DQ_FIN[p])

            _hbanks = [ps_a[0][:, 0:256], ps_a[0][:, 512:768],
                       ps_a[1][:, 0:256], ps_a[1][:, 512:768],
                       ps_s[0][:, 0:256], ps_s[1][:, 0:256]]

            def hbank(fc):
                return _hbanks[fc % 6]

            def sbank(kb):
                if kb == 0:
                    return ps_a[0][:, 0:256]
                if kb == 1:
                    return ps_a[0][:, 512:768]
                return ps_s[kb % 2][:, 0:256]

            @blk2.gpsimd
            def _(g):
                p1_dma_fence(g)
                for vc in range(8):
                    g.dma_start(out=wo_sb[:, vc * 1024:(vc + 1) * 1024],
                                in_=wo_d.ap()[vc * 128:(vc + 1) * 128, :]).then_inc(sem_w2, 16)
                for rc in range(2):
                    g.dma_start(out=v1_sb[:, rc * 4096:(rc + 1) * 4096],
                                in_=v1_d.ap()[rc * 128:(rc + 1) * 128, :]).then_inc(sem_w3, 16)
                for fc in range(32):
                    g.dma_start(out=u2_sb[:, fc * 256:(fc + 1) * 256],
                                in_=u2_d.ap()[fc * 128:(fc + 1) * 128, :]).then_inc(sem_w3, 16)
                for g_i in range(NG):
                    for kb in range(R, 4 * g_i + 4):
                        si = SI[(g_i, kb)]
                        if si >= RING:
                            g.wait_ge(sem_av, av_idx(*STREAM[si - RING]))
                        slot = si % RING
                        g.dma_start(out=vst[:, slot * 1024:(slot + 1) * 1024],
                                    in_=vscr_a[kb - R]).then_inc(vstS[slot], 16)
                    for rel in range(4):
                        kb = 4 * g_i + rel
                        it = n_av_before(g_i) + kb
                        g.wait_ge(sem_exp, it + 1)
                        g.tensor_tensor(p_sb[:, (it % 3) * 256:(it % 3 + 1) * 256],
                                        p_sb[:, (it % 3) * 256:(it % 3 + 1) * 256],
                                        msk_sb[:, rel * 256:(rel + 1) * 256],
                                        ALU.mult).then_inc(sem_msk, 1)

            @blk2.sync
            def _(s):
                p1_dma_fence(s)
                for gg in range(2):
                    s.dma_start(out=qg[:, (gg % 2) * 2048:((gg % 2) + 1) * 2048],
                                in_=qscr_a[gg]).then_inc(qsem[gg % 2], 16)
                for st_i in range(2):
                    s.dma_start(out=xo[:, st_i * 1024:(st_i + 1) * 1024],
                                in_=x_own_a[st_i * 128:(st_i + 1) * 128, :]
                                ).then_inc(sem_xo, 16)
                def out_dma(g2):
                    for st_i in range(2):
                        ti = 2 * g2 + st_i
                        s.wait_ge(sem_ln2, ti + 1)
                        s.dma_start(out=out_a[ti * 128:(ti + 1) * 128, :],
                                    in_=outb[:, (ti % 2) * 1024:((ti % 2) + 1) * 1024]
                                    ).then_inc(outd[ti % 2], 16)

                for g_i in range(NG):
                    for kb in range(R, 4 * g_i + 4):
                        si = SI[(g_i, kb)]
                        if si >= RING:
                            s.wait_ge(sem_av, av_idx(*STREAM[si - RING]))
                        slot = si % RING
                        s.dma_start(out=kst[:, slot * 1024:(slot + 1) * 1024],
                                    in_=kscr_a[kb - R]).then_inc(kstS[slot], 16)
                    if g_i > 0:
                        out_dma(g_i - 1)
                    gg = g_i + 2
                    if gg < NG:
                        s.wait_ge(sem_score, n_av_before(g_i + 1))
                        s.dma_start(out=qg[:, (gg % 2) * 2048:((gg % 2) + 1) * 2048],
                                    in_=qscr_a[gg]).then_inc(qsem[gg % 2], 16)
                    gx = g_i + 1
                    if gx < NG:
                        s.wait_ge(sem_r1h, 4 * (g_i + 1))
                        for st_i in range(2):
                            s.dma_start(out=xo[:, st_i * 1024:(st_i + 1) * 1024],
                                        in_=x_own_a[(2 * gx + st_i) * 128:(2 * gx + st_i + 1) * 128, :]
                                        ).then_inc(sem_xo, 16)
                out_dma(NG - 1)
                s.wait_ge(outd[0], 16 * 16)
                s.wait_ge(outd[1], 16 * 16)

            @blk2.tensor
            def _(t):
                p1_dma_fence(t)
                t.wait_ge(sem_w, 16 * 14)
                for rc in range(2):
                    for dc in range(8):
                        mm = t.matmul(ps_s[rc][:, 0:1],
                                      u1_sb[:, dc * 256 + rc * 128:dc * 256 + (rc + 1) * 128],
                                      ones_b[:, :],
                                      start=(dc == 0), stop=(dc == 7))
                    mm.then_inc(sem_u1c, 1)
                t.wait_ge(sem_u1ce, 2)

                def k_tile(g_i, kb, dc):
                    if kb < R:
                        return kres[:, dc * (R * 128) + kb * 128: dc * (R * 128) + (kb + 1) * 128]
                    slot = SI[(g_i, kb)] % RING
                    return kst[:, slot * 1024 + dc * 128: slot * 1024 + (dc + 1) * 128]

                def v_half(g_i, kb, h):
                    if kb < R:
                        return vres[:, kb * 1024 + h * 512: kb * 1024 + (h + 1) * 512]
                    slot = SI[(g_i, kb)] % RING
                    return vst[:, slot * 1024 + h * 512: slot * 1024 + (h + 1) * 512]

                def emit_scores(g_i, kb):
                    it = n_av_before(g_i) + kb
                    qb = (g_i % 2) * 2048
                    if kb == 0:
                        t.wait_ge(qsem[g_i % 2], 16 * (g_i // 2 + 1))
                        t.wait_ge(sem_gelu, 32 * g_i)
                    if kb == 2 and g_i > 0:
                        t.wait_ge(sem_r2h, 4 * g_i - 1)
                    if kb == 3 and g_i > 0:
                        t.wait_ge(sem_r2h, 4 * g_i)
                    if it >= 2:
                        t.wait_ge(sem_exp, it - 1)
                    if kb >= R:
                        si = SI[(g_i, kb)]
                        t.wait_ge(kstS[si % RING], 16 * (si // RING + 1))
                    for dc in range(8):
                        mm = t.matmul(sbank(kb),
                                      k_tile(g_i, kb, dc),
                                      qg[:, qb + dc * 256:qb + (dc + 1) * 256],
                                      start=(dc == 0), stop=(dc == 7))
                    mm.then_inc(sem_score, 1)

                def emit_av(g_i, kb):
                    it = n_av_before(g_i) + kb
                    if kb >= 4 * g_i:
                        t.wait_ge(sem_msk, 4 * g_i + (kb - 4 * g_i) + 1)
                    else:
                        t.wait_ge(sem_exp, it + 1)
                    if kb == 0:
                        t.wait_ge(sem_exp, it + 2)
                        t.wait_ge(sem_ge, 2 * g_i)
                        t.wait_ge(sem_gae, 8 * g_i)
                        t.wait_ge(sem_q2e, 2 * g_i)
                    if kb >= R:
                        si = SI[(g_i, kb)]
                        t.wait_ge(vstS[si % RING], 16 * (si // RING + 1))
                    pb = (it % 3) * 256
                    for st_i in range(2):
                        pt = p_sb[:, pb + st_i * 128:pb + (st_i + 1) * 128]
                        t.matmul(ps_x[st_i][:, 0:1], pt, ones_b[:, :],
                                 start=(kb == 0), stop=(kb == 4 * g_i + 3))
                        t.matmul(ps_a[st_i][:, 0:512], pt, v_half(g_i, kb, 0),
                                 start=(kb == 0), stop=(kb == 4 * g_i + 3))
                        mm = t.matmul(ps_a[st_i][:, 512:1024], pt, v_half(g_i, kb, 1),
                                      start=(kb == 0), stop=(kb == 4 * g_i + 3))
                    mm.then_inc(sem_av, 1)

                for g_i in range(NG):
                    nkb = 4 * g_i + 4
                    if g_i == 0:
                        emit_scores(0, 0)
                        emit_scores(0, 1)
                    for kb in range(nkb):
                        emit_av(g_i, kb)
                        if kb + 2 < nkb:
                            emit_scores(g_i, kb + 2)
                    for i in range(16):
                        st_i, vc = divmod(i, 8)
                        if i % 2 == 0:
                            t.wait_ge(sem_gae, 8 * g_i + i // 2 + 1)
                        if i == 2:
                            t.wait_ge(sem_ge, 2 * g_i + 1)
                        if i == 3:
                            t.wait_ge(sem_ge, 2 * g_i + 2)
                        if 16 * g_i + i >= 4:
                            t.wait_ge(sem_trpe, 16 * g_i + i - 3)
                        t.transpose((ps_s + ps_x)[i % 4][:, 0:128],
                                    a_sb[:, st_i * 1024 + vc * 128:st_i * 1024 + (vc + 1) * 128],
                                    idf[:, :]).then_inc(sem_trp, 1)
                    if g_i == 0:
                        t.wait_ge(sem_w2, 16 * 8)
                    for st_i in range(2):
                        if st_i == 0:
                            t.wait_ge(sem_gae, 8 * g_i + 4)
                        for h in range(2):
                            wu = 4 * g_i + st_i * 2 + h
                            wo_out = (ps_a[0][:, h * 512:(h + 1) * 512] if st_i == 0
                                      else ps_s[h][:, 0:512])
                            if wu >= 2:
                                t.wait_ge(sem_r1h, wu - 1)
                            for vc in range(8):
                                t.wait_ge(sem_trpe, 16 * g_i + st_i * 8 + vc + 1)
                                mm = t.matmul(wo_out,
                                              at_sb[:, vc * 256 + st_i * 128:vc * 256 + (st_i + 1) * 128],
                                              wo_sb[:, vc * 1024 + h * 512:vc * 1024 + (h + 1) * 512],
                                              start=(vc == 0), stop=(vc == 7))
                            mm.then_inc(sem_wo, 1)
                    for st_i in range(2):
                        for dc in range(8):
                            i = st_i * 8 + dc
                            if i == 0:
                                t.wait_ge(sem_r1h, 4 * g_i + 2)
                            if i == 2:
                                t.wait_ge(sem_r1h, 4 * g_i + 3)
                            if i == 3:
                                t.wait_ge(sem_r1h, 4 * (g_i + 1))
                            if 16 * g_i + i >= 4:
                                t.wait_ge(sem_xte, 16 * g_i + i - 3)
                            t.transpose((ps_x + ps_s)[i % 4][:, 0:128],
                                        x1[:, st_i * 1024 + dc * 128:st_i * 1024 + (dc + 1) * 128],
                                        idf[:, :]).then_inc(sem_xt, 1)
                    for rc in range(2):
                        t.wait_ge(sem_xte, 16 * g_i + 13 + rc)
                        if 2 * g_i + rc >= 2:
                            t.wait_ge(sem_p1e, 2 * g_i + rc - 1)
                        for dc in range(8):
                            t.wait_ge(sem_xte, 16 * g_i + min(9 + dc, 16))
                            mm = t.matmul(ps_x[rc][:, 0:256],
                                          u1_sb[:, dc * 256 + rc * 128:dc * 256 + (rc + 1) * 128],
                                          x1t[:, dc * 256:(dc + 1) * 256],
                                          start=(dc == 0), stop=(dc == 7))
                        mm.then_inc(sem_p1, 1)
                    t.wait_ge(sem_st1, g_i + 1)
                    for st_i in range(2):
                        bank = ps_a[1 - st_i]
                        t.transpose(bank[0:1, 0:128],
                                    st1[:, st_i * 12 + 7:st_i * 12 + 8],
                                    idf[:, :]).then_inc(sem_stt, 1)
                        t.transpose(bank[0:1, 512:640],
                                    st1[:, st_i * 12 + 8:st_i * 12 + 9],
                                    idf[:, :]).then_inc(sem_stt, 1)
                    t.wait_ge(sem_ste, 2 * (g_i + 1))
                    t.matmul(ps_a[1][:, 0:256], ones_r[0:1, 0:128],
                             stT[0:1, 0:256], start=True, stop=True).then_inc(sem_rep, 1)
                    t.matmul(ps_a[1][:, 512:768], ones_r[0:1, 0:128],
                             stT[0:1, 256:512], start=True, stop=True).then_inc(sem_rep, 1)
                    def q2_partial(fc):
                        t.wait_ge(sem_gelu, 32 * g_i + fc + 1)
                        if fc == 0 and g_i > 0:
                            t.wait_ge(sem_q2e, 2 * g_i)
                        for rc in range(2):
                            mm = t.matmul(ps_x[rc][:, 0:256],
                                          u2_sb[:, fc * 256 + rc * 128:fc * 256 + (rc + 1) * 128],
                                          h_sb[:, (fc % 8) * 256:(fc % 8 + 1) * 256],
                                          start=(fc == 0), stop=(fc == 31))
                        mm.then_inc(sem_q2p, 1)

                    if g_i == 0:
                        t.wait_ge(sem_w3, 16 * 34)
                    t.wait_ge(sem_p1e, 2 * (g_i + 1))
                    for fc in range(32):
                        if fc >= 6:
                            t.wait_ge(sem_gelu, 32 * g_i + fc - 5)
                        for rc in range(2):
                            mm = t.matmul(hbank(fc),
                                          v1_sb[:, rc * 4096 + fc * 128:rc * 4096 + (fc + 1) * 128],
                                          p1_sb[:, rc * 256:(rc + 1) * 256],
                                          start=(rc == 0), stop=(rc == 1))
                        mm.then_inc(sem_h, 1)
                        if fc >= 5:
                            q2_partial(fc - 5)
                    for fc in range(27, 32):
                        q2_partial(fc)
                    if g_i + 1 < NG:
                        emit_scores(g_i + 1, 0)
                        emit_scores(g_i + 1, 1)
                    t.wait_ge(sem_q2e, 2 * (g_i + 1))
                    for st_i in range(2):
                        for h in range(2):
                            fu = 4 * g_i + st_i * 2 + h
                            if fu >= 2:
                                t.wait_ge(sem_r2h, fu - 1)
                            for rc in range(2):
                                mm = t.matmul(ps_s[fu % 2][:, 0:512],
                                              q2_sb[:, rc * 256 + st_i * 128:rc * 256 + (st_i + 1) * 128],
                                              v2_sb[:, rc * 1024 + h * 512:rc * 1024 + (h + 1) * 512],
                                              start=(rc == 0), stop=(rc == 1))
                            mm.then_inc(sem_f, 1)

            @blk2.scalar
            def _(sc):
                p1_dma_fence(sc)
                def ln_sq(tile_ap, scratch_ap, ssq_ap):
                    sc.activation(scratch_ap, tile_ap, AF.Square, accum_out=ssq_ap)

                def emit_exp(g_i, kb):
                    it = n_av_before(g_i) + kb
                    sc.wait_ge(sem_score, it + 1)
                    if it >= 3:
                        sc.wait_ge(sem_av, it - 2)
                    sc.activation(p_sb[:, (it % 3) * 256:(it % 3 + 1) * 256],
                                  sbank(kb), AF.Exp,
                                  bias=shift_s[:, 0:1], scale=SCALE).then_inc(sem_exp, 1)

                def sqrt2_pair(g2):
                    sc.wait_ge(sem_vb, g2 + 1)
                    sc.activation(bass.AP(st1, 6, [[24, 128], [12, 2]]),
                                  bass.AP(st1, 5, [[24, 128], [12, 2]]),
                                  AF.Sqrt, bias=eps_s[:, 0:1]).then_inc(sem_sd2, 1)

                for g_i in range(NG):
                    nkb = 4 * g_i + 4
                    if g_i == 0:
                        for kb in range(4):
                            emit_exp(0, kb)
                    for idx, kb in enumerate(range(6, nkb)):
                        emit_exp(g_i, kb)
                        if idx == 1 and g_i >= 1:
                            sqrt2_pair(g_i - 1)
                    sc.wait_ge(sem_av, n_av_before(g_i + 1))
                    if g_i > 0:
                        sc.wait_ge(sem_trp, 16 * g_i)
                    for st_i in range(2):
                        for p in range(4):
                            sc.activation(a_sb[:, st_i * 1024 + p * 256:st_i * 1024 + (p + 1) * 256],
                                          ps_a[st_i][:, p * 256:(p + 1) * 256],
                                          AF.Copy).then_inc(sem_gae, 1)
                    sc.drain()
                    sc.wait_ge(sem_q2p, 32 * g_i)
                    for st_i in range(2):
                        sc.wait_ge(sem_r1h, 4 * g_i + 2 * (st_i + 1))
                        ln_sq(x1[:, st_i * 1024:(st_i + 1) * 1024],
                              h_sb[:, st_i * 1024:(st_i + 1) * 1024],
                              st1[:, 1 + st_i * 12:2 + st_i * 12])
                        sc.drain().then_inc(sem_sq1, 1)
                    sc.wait_ge(sem_va, g_i + 1)
                    sc.activation(bass.AP(st1, 6, [[24, 128], [12, 2]]),
                                  bass.AP(st1, 5, [[24, 128], [12, 2]]), AF.Sqrt,
                                  bias=eps_s[:, 0:1]).then_inc(sem_sd, 1)
                    for fc in range(32):
                        sc.wait_ge(sem_h, 32 * g_i + fc + 1)
                        if fc >= 8:
                            sc.wait_ge(sem_q2p, 32 * g_i + fc - 7)
                        sc.activation(h_sb[:, (fc % 8) * 256:(fc % 8 + 1) * 256],
                                      hbank(fc), AF.Gelu_apprx_tanh,
                                      bias=b1_sb[:, fc:fc + 1]).then_inc(sem_gelu, 1)
                    if g_i > 0:
                        sc.wait_ge(sem_f, 4 * g_i)
                    sc.wait_ge(sem_q2p, 32 * (g_i + 1))
                    for rc in range(2):
                        sc.activation(q2_sb[:, rc * 256:(rc + 1) * 256],
                                      ps_x[rc][:, 0:256],
                                      AF.Copy).then_inc(sem_q2e, 1)
                    if g_i + 1 < NG:
                        for kb in range(6):
                            emit_exp(g_i + 1, kb)
                sqrt2_pair(NG - 1)

            @blk2.vector
            def _(v):
                p1_dma_fence(v)
                for rc in range(2):
                    v.wait_ge(sem_u1c, rc + 1)
                    v.tensor_copy(u1cs[:, rc:rc + 1],
                                  ps_s[rc][:, 0:1]).then_inc(sem_u1ce, 1)

                def stp(c):
                    return bass.AP(st1, c, [[24, 128], [12, 2]])

                def ln_tail_paired(sem_va_, sem_sd_, need, sem_st_):
                    v.tensor_scalar(stp(2), stp(0), -1.0 / 1024.0, None, ALU.mult)
                    v.tensor_scalar(stp(3), stp(1), 1.0 / 1024.0, None, ALU.mult)
                    v.drain()
                    v.tensor_tensor(stp(4), stp(2), stp(2), ALU.mult)
                    v.drain()
                    v.tensor_tensor(stp(5), stp(3), stp(4), ALU.subtract)
                    v.drain().then_inc(sem_va_, 1)
                    v.wait_ge(sem_sd_, need)
                    v.reciprocal(stp(7), stp(6))
                    v.drain()
                    v.tensor_tensor(stp(8), stp(2), stp(7), ALU.mult)
                    v.drain().then_inc(sem_st_, 1)

                for g_i in range(NG):
                    nkb = 4 * g_i + 4
                    v.wait_ge(sem_av, n_av_before(g_i + 1))
                    for st_i in range(2):
                        v.reciprocal(recip[:, st_i:st_i + 1],
                                     ps_x[st_i][:, 0:1]).then_inc(sem_ge, 1)
                    if g_i > 0:
                        v.wait_ge(sem_wo, 4 * g_i)
                    for i in range(16):
                        st_i, vc = divmod(i, 8)
                        v.wait_ge(sem_trp, 16 * g_i + i + 1)
                        v.tensor_copy(at_sb[:, vc * 256 + st_i * 128:vc * 256 + (st_i + 1) * 128],
                                      (ps_s + ps_x)[i % 4][:, 0:128]).then_inc(sem_trpe, 1)
                    v.drain()
                    if g_i > 0:
                        v.wait_ge(sem_ln2, 2 * g_i)
                        v.wait_ge(sem_sq2, 2 * g_i)
                    v.wait_ge(sem_xo, 32 * (g_i + 1))
                    for st_i in range(2):
                        for h in range(2):
                            wu = 4 * g_i + st_i * 2 + h
                            v.wait_ge(sem_wo, wu + 1)
                            wo_src = (ps_a[0][:, h * 512:(h + 1) * 512] if st_i == 0
                                      else ps_s[h][:, 0:512])
                            v.scalar_tensor_tensor(
                                x1[:, st_i * 1024 + h * 512:st_i * 1024 + (h + 1) * 512],
                                wo_src,
                                recip[:, st_i:st_i + 1],
                                xo[:, st_i * 1024 + h * 512:st_i * 1024 + (h + 1) * 512],
                                ALU.mult, ALU.add).then_inc(sem_r1h, 1)
                    v.drain()
                    if g_i > 0:
                        v.wait_ge(sem_p1, 2 * g_i)
                    for i in range(16):
                        st_i, dc = divmod(i, 8)
                        v.wait_ge(sem_xt, 16 * g_i + i + 1)
                        v.tensor_copy(x1t[:, dc * 256 + st_i * 128:dc * 256 + (st_i + 1) * 128],
                                      (ps_x + ps_s)[i % 4][:, 0:128]).then_inc(sem_xte, 1)
                    for st_i in range(2):
                        o = st_i * 12
                        v.wait_ge(sem_sq1, 2 * g_i + st_i + 1)
                        v.reduce_sum(st1[:, 0 + o:1 + o],
                                     x1[:, st_i * 1024:(st_i + 1) * 1024],
                                     axis=mybir.AxisListType.X)
                    v.drain()
                    ln_tail_paired(sem_va, sem_sd, g_i + 1, sem_st1)
                    for st_i in range(2):
                        v.wait_ge(sem_stt, 4 * g_i + 2 * (st_i + 1))
                        bank = ps_a[1 - st_i]
                        v.tensor_copy(stT[0:1, st_i * 128:(st_i + 1) * 128],
                                      bank[0:1, 0:128])
                        v.tensor_copy(stT[0:1, 256 + st_i * 128:256 + (st_i + 1) * 128],
                                      bank[0:1, 512:640]).then_inc(sem_ste, 1)
                    v.drain()
                    for st_i in range(2):
                        o = st_i * 12
                        v.wait_ge(sem_xt, 16 * g_i + 8 * (st_i + 1))
                        v.tensor_scalar(x1[:, st_i * 1024:(st_i + 1) * 1024],
                                        x1[:, st_i * 1024:(st_i + 1) * 1024],
                                        st1[:, 2 + o:3 + o], st1[:, 7 + o:8 + o],
                                        ALU.add, ALU.mult)
                        v.drain()
                        v.tensor_tensor(x1[:, st_i * 1024:(st_i + 1) * 1024],
                                        x1[:, st_i * 1024:(st_i + 1) * 1024],
                                        b2_sb[:, :], ALU.add)
                    v.drain()
                    v.wait_ge(sem_rep, 2 * (g_i + 1))
                    v.tensor_copy(rep_sb[:, 0:256], ps_a[1][:, 0:256])
                    v.tensor_copy(rep_sb[:, 256:512], ps_a[1][:, 512:768])
                    v.drain()
                    if g_i > 0:
                        v.wait_ge(sem_h, 32 * g_i)
                    for rc in range(2):
                        v.wait_ge(sem_p1, 2 * g_i + rc + 1)
                        v.tensor_tensor(a_sb[:, rc * 256:(rc + 1) * 256],
                                        ps_x[rc][:, 0:256],
                                        rep_sb[:, 0:256], ALU.mult)
                        v.drain()
                        v.scalar_tensor_tensor(p1_sb[:, rc * 256:(rc + 1) * 256],
                                               rep_sb[:, 256:512],
                                               u1cs[:, rc:rc + 1],
                                               a_sb[:, rc * 256:(rc + 1) * 256],
                                               ALU.mult, ALU.add).then_inc(sem_p1e, 1)
                    for st_i in range(2):
                        for h in range(2):
                            fu = 4 * g_i + st_i * 2 + h
                            v.wait_ge(sem_f, fu + 1)
                            v.tensor_tensor(x1[:, st_i * 1024 + h * 512:st_i * 1024 + (h + 1) * 512],
                                            ps_s[fu % 2][:, 0:512],
                                            x1[:, st_i * 1024 + h * 512:st_i * 1024 + (h + 1) * 512],
                                            ALU.add).then_inc(sem_r2h, 1)
                    v.drain()
                    for st_i in range(2):
                        o = st_i * 12
                        v.tensor_tensor(a_sb[:, st_i * 1024:(st_i + 1) * 1024],
                                        x1[:, st_i * 1024:(st_i + 1) * 1024],
                                        x1[:, st_i * 1024:(st_i + 1) * 1024],
                                        ALU.mult)
                        v.drain()
                        v.reduce_sum(st1[:, 1 + o:2 + o],
                                     a_sb[:, st_i * 1024:(st_i + 1) * 1024],
                                     axis=mybir.AxisListType.X)
                        v.drain().then_inc(sem_sq2, 1)
                    for st_i in range(2):
                        o = st_i * 12
                        v.reduce_sum(st1[:, 0 + o:1 + o],
                                     x1[:, st_i * 1024:(st_i + 1) * 1024],
                                     axis=mybir.AxisListType.X)
                    v.drain()
                    ln_tail_paired(sem_vb, sem_sd2, g_i + 1, sem_st2)
                    for st_i in range(2):
                        o = st_i * 12
                        ti = 2 * g_i + st_i
                        if ti >= 2:
                            v.wait_ge(outd[ti % 2], 16 * (ti // 2))
                        v.tensor_scalar(outb[:, (ti % 2) * 1024:((ti % 2) + 1) * 1024],
                                        x1[:, st_i * 1024:(st_i + 1) * 1024],
                                        st1[:, 2 + o:3 + o], st1[:, 7 + o:8 + o],
                                        ALU.add, ALU.mult)
                        v.drain().then_inc(sem_ln2, 1)

    return nc


_NC_CACHE = {}


def kernel(L0_attn_x, L0_attn_Wq, L0_attn_Wk, L0_attn_Wv, L0_attn_Wo,
           L0_ffn_U1, L0_ffn_V1, L0_ffn_b1, L0_ffn_U2, L0_ffn_V2, L0_ffn_b2):
    x = np.ascontiguousarray(np.asarray(L0_attn_x, dtype=np.float32))
    B, Sq, D = x.shape
    assert (B, Sq, D) == (4, 8192, 1024)

    wq = np.ascontiguousarray(np.asarray(L0_attn_Wq, np.float32))
    wk = np.ascontiguousarray(np.asarray(L0_attn_Wk, np.float32))
    wv = np.ascontiguousarray(np.asarray(L0_attn_Wv, np.float32))
    wo = np.ascontiguousarray(np.asarray(L0_attn_Wo, np.float32))
    u1 = np.ascontiguousarray(np.asarray(L0_ffn_U1, np.float32))
    v1 = np.ascontiguousarray(np.asarray(L0_ffn_V1, np.float32))
    b1 = np.asarray(L0_ffn_b1, np.float32)
    u2 = np.ascontiguousarray(np.asarray(L0_ffn_U2, np.float32))
    v2 = np.ascontiguousarray(np.asarray(L0_ffn_V2, np.float32))
    b2 = np.asarray(L0_ffn_b2, np.float32)

    b1t = np.ascontiguousarray(b1.reshape(32, 128).T)
    b2b = np.ascontiguousarray(np.broadcast_to(b2, (128, 1024)))
    ident = np.eye(128, dtype=np.float32)

    tri = (np.arange(128)[:, None] <= np.arange(128)[None, :]).astype(np.float32)
    full = np.ones((128, 128), np.float32)
    zero = np.zeros((128, 128), np.float32)

    def mask_for(par):
        m = np.zeros((128, 1024), np.float32)
        for rel in range(4):
            for s in range(2):
                d = 2 * s + par
                blkm = full if rel < d else (tri if rel == d else zero)
                m[:, rel * 256 + s * 128: rel * 256 + (s + 1) * 128] = blkm
        return m

    masks = [mask_for(0), mask_for(1)]

    if "nc" not in _NC_CACHE:
        _NC_CACHE["nc"] = build_program()
    nc = _NC_CACHE["nc"]

    in_maps = []
    for c in range(8):
        b, par = c >> 1, c & 1
        xb = x[b]
        x_own = np.ascontiguousarray(
            xb.reshape(32, 2, 128, 1024)[:, par].reshape(4096, 1024))
        in_maps.append({
            "x_seq": xb, "x_own": x_own,
            "wq": wq, "wk": wk, "wv": wv, "wo": wo,
            "u1": u1, "v1": v1, "b1t": b1t, "u2": u2, "v2": v2, "b2b": b2b,
            "masks": masks[par], "ident": ident,
        })

    res = run_bass_kernel_spmd(nc, in_maps, core_ids=list(range(8)))

    out = np.empty((4, 8192, 1024), np.float32)
    for c in range(8):
        b, par = c >> 1, c & 1
        oc = res.results[c]["out"]
        out[b].reshape(32, 2, 128, 1024)[:, par] = oc.reshape(32, 128, 1024)
    return out
